# version 01 (final)
# baseline (speedup 1.0000x reference)
import math
from contextlib import ExitStack

import numpy as np
import ml_dtypes

import concourse.bass as bass
import concourse.mybir as mybir
from concourse.bass_utils import run_bass_kernel_spmd

F32 = mybir.dt.float32
BF16 = mybir.dt.bfloat16
AF = mybir.ActivationFunctionType
ALU = mybir.AluOpType
AX = mybir.AxisListType
bfdt = ml_dtypes.bfloat16

T = 2048
D = 1024
NT = T // 128
NC = D // 128
LAM_INIT = 0.8 - 0.6 * math.exp(-0.3 * 11)
LN_EPS = 1e-5
CTX = (21, 41)
PADMAX = 20
WCOLS = 496
SEG_BASE = (0, 168)
VSTRIDE = 132

DEBUG_STOP = None
DEBUG_TAPS = False
SERIALIZE_DMA = True

ENGINES = ("tensor", "vector", "scalar", "sync", "gpsimd")


class Prog:

    def __init__(self, nc):
        self.nc = nc
        self.ops = []
        self.barriers = []

    def op(self, engine, emit, reads=(), writes=(), dma=False, special=False):
        self.ops.append(dict(engine=engine, emit=emit, reads=tuple(reads),
                             writes=tuple(writes), dma=dma, special=special))

    def t(self, emit, reads=(), writes=()):
        self.op("tensor", emit, reads, writes)

    def v(self, emit, reads=(), writes=()):
        self.op("vector", emit, reads, writes)

    def vs(self, emit, reads=(), writes=()):
        self.op("vector", emit, reads, writes, special=True)

    def s(self, emit, reads=(), writes=()):
        self.op("scalar", emit, reads, writes)

    def dma(self, queue, emit, reads=(), writes=()):
        self.op(queue, emit, reads, writes, dma=True)

    def barrier(self):
        self.barriers.append(len(self.ops))

    def finalize(self, es):
        nc = self.nc
        n = len(self.ops)
        stream = [o["engine"] + ("_dma" if o["dma"] else "") for o in self.ops]
        streams = sorted(set(stream))
        last_writer, readers = {}, {}
        deps = [set() for _ in range(n)]
        bptr = 0
        last_on_stream = {}
        frontier = {}
        for i, o in enumerate(self.ops):
            while bptr < len(self.barriers) and self.barriers[bptr] <= i:
                frontier = dict(last_on_stream)
                bptr += 1
            for r in o["reads"]:
                if r in last_writer:
                    deps[i].add(last_writer[r])
            for r in o["writes"]:
                if r in last_writer:
                    deps[i].add(last_writer[r])
                for j in readers.get(r, ()):
                    deps[i].add(j)
            for s_, j in frontier.items():
                if s_ != stream[i]:
                    deps[i].add(j)
            for r in o["writes"]:
                last_writer[r] = i
                readers[r] = []
            for r in o["reads"]:
                readers.setdefault(r, []).append(i)
            last_on_stream[stream[i]] = i
        need = [set() for _ in range(n)]
        for i in range(n):
            for j in deps[i]:
                if stream[j] != stream[i]:
                    need[i].add(j)
        if SERIALIZE_DMA:
            lastdma = {}
            for i, o in enumerate(self.ops):
                if o["dma"]:
                    if stream[i] in lastdma:
                        need[i].add(lastdma[stream[i]])
                    lastdma[stream[i]] = i
        inc_needed = [o["dma"] for o in self.ops]
        for i in range(n):
            for j in need[i]:
                inc_needed[j] = True
        count_after = [0] * n
        cnt = {s_: 0 for s_ in streams}
        for i, o in enumerate(self.ops):
            if inc_needed[i]:
                cnt[stream[i]] += 16 if o["dma"] else 1
            count_after[i] = cnt[stream[i]]
        sems = {s_: es.enter_context(nc.semaphore("sem_" + s_))
                for s_ in streams if cnt[s_] > 0}
        ops_by_engine = {e: [] for e in ENGINES}
        for i, o in enumerate(self.ops):
            ops_by_engine[o["engine"]].append(i)
        sat = {e: {} for e in ENGINES}
        block = es.enter_context(nc.Block())
        prog = self

        def engine_body(eng_obj, engine_name):
            prev_special = False
            prev_writes = ()
            for i in ops_by_engine[engine_name]:
                o = prog.ops[i]
                if engine_name == "vector" and not o["dma"]:
                    raw1 = any(r in prev_writes for r in o["reads"])
                    if prev_special != o["special"] or raw1:
                        eng_obj.tensor_copy(prog.spacer_ap[:, 0:1],
                                            prog.spacer_ap[:, 1:2])
                    prev_special = o["special"]
                    prev_writes = o["writes"]
                waits = {}
                for j in need[i]:
                    sj = stream[j]
                    c = count_after[j]
                    if sat[engine_name].get(sj, 0) < c:
                        waits[sj] = max(waits.get(sj, 0), c)
                for sj, c in sorted(waits.items()):
                    eng_obj.wait_ge(sems[sj], c)
                    sat[engine_name][sj] = c
                inst = o["emit"](eng_obj)
                if inc_needed[i]:
                    inst.then_inc(sems[stream[i]], 16 if o["dma"] else 1)

        for name in ENGINES:
            if ops_by_engine[name]:
                getattr(block, name)(lambda eng, _n=name: engine_body(eng, _n))


def build_program(debug_stop=None, taps=False):
    nc = bass.Bass("TRN2", target_bir_lowering=False)
    es = ExitStack()
    p = Prog(nc)

    def din(name, shape, dtype=F32):
        return nc.dram_tensor(name, list(shape), dtype, kind="ExternalInput").ap()

    xqT = din("xqT", (D, T))
    WinT = din("WinT", (D, D))
    WattnT = din("WattnT", (D, WCOLS))
    WoutT = din("WoutT", (D, D))
    swin = din("swin", (1, 2))
    wiT = din("wiT", (D, 4096))
    bi_in = din("bi_in", (1, 4096))
    woT = din("woT", (4096, D))
    bo_in = din("bo_in", (1, D))
    lAspA = din("lAspA", (D, 96))
    lAspB = din("lAspB", (D, 32))
    btsrc_in = {"eA": din("bteA", (96, 4096)), "oA": din("btoA", (96, 4096)),
                "eB": din("bteB", (32, 4096)), "oB": din("btoB", (32, 4096))}
    gwT = din("gwT", (D, 8))
    gbrow = din("gbrow", (1, 8))
    c0row = din("c0row", (1, D))
    wmod0 = din("wmod0", (D, 6 * D))
    WQK = {"q1": din("Wq1", (D, D)), "q2": din("Wq2", (D, D)),
           "k1": din("Wk1", (D, D)), "k2": din("Wk2", (D, D))}
    Wv = din("Wv", (D, 2 * D))
    Wo1 = din("Wo1", (2 * D, D))
    lqk = {k: din(k, (16, 64)) for k in ("lq1", "lk1", "lq2", "lk2")}
    W1g = din("W1g", (4096, D))
    W2Tg = din("W2Tg", (4096, D))
    b1row = din("b1row", (1, 4096))
    b2row = din("b2row", (1, D))
    c1row = din("c1row", (1, D))
    wmod1 = din("wmod1", (D, 6 * D))
    ident = din("ident", (128, 128), BF16)
    onesr_in = din("onesr", (1, 128), BF16)
    onesrf_in = din("onesrf", (1, 128), F32)
    onesc_in = din("onesc", (128, 1), BF16)
    headsel = din("headsel", (D, 16), BF16)
    mask_in = {21: din("mask21", (128, 148), BF16),
               41: din("mask41", (128, 168), BF16)}
    tri_in = din("tri", (128, 128), BF16)
    idf_in = din("idf", (128, 128), F32)
    acst_in = din("acst_in", (128, 3), F32)

    yout = nc.dram_tensor("y", [T, D], F32, kind="ExternalOutput").ap()
    wscr = nc.dram_tensor("wscr", [T * WCOLS + 2048], BF16, kind="Internal").ap()
    xscr = nc.dram_tensor("xscr", [(T + 2 * PADMAX) * D], BF16, kind="Internal").ap()
    mscr = nc.dram_tensor("mscr", [16, 128, T], BF16, kind="Internal").ap()

    taps_d = {}
    if taps:
        def tapt(name, shape, dtype):
            taps_d[name] = nc.dram_tensor(name, list(shape), dtype,
                                          kind="ExternalOutput").ap()
        tapt("t_h", (D, T), BF16)
        tapt("t_dbg", (128, 128), F32)
        tapt("t_xin", (D, T), BF16)
        tapt("t_w", (128, WCOLS), BF16)
        tapt("t_x0", (D, T), F32)
        tapt("t_g", (128, 128), F32)
        tapt("t_x1", (D, T), F32)
        tapt("t_pv", (128, 128), F32)
        tapt("t_x2", (D, T), F32)

    def sb(name, shape, dtype):
        return es.enter_context(nc.sbuf_tensor(name, list(shape), dtype))

    x_st = [sb(f"x{c}", (128, T), F32) for c in range(NC)]
    bufA = [sb(f"bufA{c}", (128, T), BF16) for c in range(NC)]
    bufB = [sb(f"bufB{c}", (128, T), BF16) for c in range(NC)]
    bufC = [sb(f"bufC{c}", (128, T), BF16) for c in range(NC)]
    wbuf = [sb(f"wbuf{c}", (128, T), BF16) for c in range(6)]
    f32scr = sb("f32scr", (128, 1024), F32)
    pb = [es.enter_context(nc.psum_tensor(f"pb{i}", [128, 512], F32))
          for i in range(6)]
    pt = [es.enter_context(nc.psum_tensor(f"pt{i}", [128, 1024], BF16))
          for i in range(2)]
    PB = [f"pb{i}" for i in range(6)]
    PT = [f"pt{i}" for i in range(2)]
    id_sb = sb("id_sb", (128, 128), BF16)
    idf_sb = sb("idf_sb", (128, 128), F32)
    acst = sb("acst", (128, 3), F32)
    onesr_sb = sb("onesr_sb", (1, 128), BF16)
    onesrf_sb = sb("onesrf_sb", (1, 128), F32)
    onesc_sb = sb("onesc_sb", (128, 1), BF16)
    ones512 = sb("ones512", (1, 512), BF16)
    zrow = sb("zrow", (1, 512), BF16)
    hsel_sb = [sb(f"hsel{c}", (128, 16), BF16) for c in range(NC)]
    mask_sb = {21: sb("m21", (128, 148), BF16), 41: sb("m41", (128, 168), BF16)}
    tri_sb = sb("tri_sb", (128, 128), BF16)
    masks_sc = {21: sb("m21s", (128, 148), BF16), 41: sb("m41s", (128, 168), BF16)}
    mod_sb = sb("mod_sb", (128, 64), F32)
    gclamp = sb("gclamp", (128, 24), F32)
    xcast = sb("xcast", (128, 512), BF16)
    xsq = sb("xsq", (128, 512), BF16)
    silu_f = sb("silu_f", (128, 8), F32)
    silu_b = sb("silu_b", (128, 8), BF16)
    silu2 = sb("silu2", (128, 8), F32)
    spacer = sb("spacer", (1, 2), F32)
    p.spacer_ap = spacer
    dbgA = sb("dbgA", (128, 8), F32)
    ctmp = sb("ctmp", (128, 8), F32)
    xins = sb("xins", (128, 128), BF16)
    xtm = sb("xtm", (128, D), BF16)
    wmax = sb("wmax", (128, 16, 1), F32)
    wsum = sb("wsum", (128, 16, 1), F32)
    wt_s = sb("wt_s", (128, WCOLS), F32)
    sw_sb = sb("sw_sb", (1, 2), F32)
    swtmp = sb("swtmp", (1, 2), F32)
    swf2 = sb("swf2", (1, 2), F32)
    swb = sb("swb", (1, 2), BF16)
    swcol = sb("swcol", (128, 2), F32)
    gwT_sb = sb("gwT_sb", (128, 64), BF16)
    gb_bc = sb("gb_bc", (1, 8), BF16)
    Gall = sb("Gall", (128, 128), F32)
    gtile = sb("gtile", (128, 8), F32)
    grank = sb("grank", (128, 8), F32)
    gsel = sb("gsel", (128, 8), F32)
    gvs = sb("gvs", (128, 1), F32)
    gmax = sb("gmax", (128, 1), F32)
    bo_fm = sb("bo_fm", (128, 8), F32)
    gbbo = sb("gbbo", (128, 8), F32)
    gbboT = sb("gbboT", (8, 128), F32)
    lqk_sb = {k: sb(k + "_sb", (16, 64), F32) for k in lqk}
    lam_col = sb("lam_col", (16, 1), F32)
    ltmp = sb("ltmp", (16, 64), F32)
    ldot = sb("ldot", (16, 2), F32)
    lamr = sb("lamr", (1, 16), F32)
    lamb = sb("lamb", (128, 2), F32)
    nmax = {k: sb("nmax_" + k, (16, 1), F32) for k in ("q1", "k1", "q2", "k2")}
    nred = sb("nred", (16, 1), F32)
    nred4 = sb("nred4", (16, 4), F32)
    mrow_ = sb("mrow_", (1, 16), F32)
    biascol = sb("biascol", (128, 4), F32)
    gnm = sb("gnm", (128, 1), F32)
    gnv = sb("gnv", (128, 1), F32)
    inv1 = sb("inv1", (128, 1), F32)
    inv2 = sb("inv2", (128, 1), F32)
    negc2 = sb("negc2", (128, 1), F32)
    mgst = sb("mgst", (128, 128), BF16)
    b1fm = sb("b1fm", (128, 32), F32)
    b1bf = sb("b1bf", (128, 32), BF16)
    befc = sb("befc", (128, 8), F32)
    befT = sb("befT", (8, 128), F32)
    b2fm = sb("b2fm", (128, 8), F32)
    otile = f32scr[:, 0:1024]

    zpad = bufB[6][0:PADMAX, 0:D]
    wt_f = f32scr[:, 0:496]
    wt_b = bufB[6][:, 1024:1520]
    wt_b = bufB[6][:, 1024:1520]
    aall = bufA[2][:, 0:640]
    astage = bufA[2][:, 640:808]
    at1 = bufA[3][:, 0:128]
    at2 = bufA[3][0:40, 128:256]
    xw1 = {21: bufA[0][:, 0:D], 41: bufA[0][:, D:2 * D]}
    xw2 = {21: bufA[1][0:20, 0:D], 41: bufA[1][0:40, D:2 * D]}
    M_bc = bufB[4][:]
    R_bc = bufB[5][:]
    mr512 = bufB[6][0:1, 1536:2048]
    rs512 = bufB[7][0:1, 1024:1536]
    msq512 = bufB[7][0:1, 1536:2048]
    basebf = bufC[0][:, 0:512]
    tmpbf = bufC[0][:, 512:1024]
    accbf = bufC[0][:, 1024:1536]
    accfm = bufC[0][:, 1536:2048]
    btq = {"eA": bufC[4][0:96, 0:1024], "oA": bufC[4][0:96, 1024:2048],
           "eB": bufC[5][0:32, 0:1024], "oB": bufC[5][0:32, 1024:2048]}
    sA = bufC[6][0:96, :]
    sB = bufC[7][0:32, :]
    lAa = bufC[1][:, 0:768]
    lAb = bufC[1][:, 768:1024]
    gsr_bf = bufC[2][0:1, 0:2048]
    biq = bufC[3][0:1, 0:1024]
    gbbo_fl = bufC[3][0:1, 1024:2048]
    bef_fl = bufC[4][0:1, 0:1024]
    sqs = bufC[6][:, 0:512]
    esb = [bufC[6][:, 512:1024], bufC[6][:, 1024:1536]]
    estg = [bufC[7][:, 0:128], bufC[7][:, 128:256]]
    lnrow = f32scr[0:1, 512:1024]
    gnin = f32scr[:, 0:128]
    gnd = f32scr[:, 128:256]
    gnsq = f32scr[:, 256:384]

    def vt_sl(vc, kt, wid):
        slot = 2 * vc + kt // 8
        c0 = (kt % 8) * VSTRIDE
        return bufB[slot][:, c0:c0 + wid]

    def weT_sl(bb, c0, c1):
        return bufC[bb // 2][:, (bb % 2) * D + c0:(bb % 2) * D + c1]

    w1s = [wbuf[0][:, 0:512], wbuf[0][:, 512:1024]]
    w2s = [wbuf[0][:, 1024:1536], wbuf[0][:, 1536:2048]]

    def ld(dst, src, key, queue="sync"):
        p.dma(queue, lambda e, d=dst, s=src: e.dma_start(out=d, in_=s),
              reads=(), writes=(key,))

    zconst = nc.const_aps.scalar_like(0.0, acst[:, 0:1], dtype=F32)
    p.v(lambda e: e.memset(zconst, 0.0), writes=("zconst",))
    ld(id_sb[:], ident, "id_sb")
    ld(idf_sb[:], idf_in, "idf")
    ld(acst[:], acst_in, "acst")
    ld(onesr_sb[:], onesr_in, "onesr")
    ld(onesrf_sb[:], onesrf_in, "onesrf")
    ld(onesc_sb[:], onesc_in, "onesc")
    ld(tri_sb[:], tri_in, "tri")
    for cw in CTX:
        ld(mask_sb[cw][:], mask_in[cw], f"mask{cw}")
    for c in range(NC):
        ld(hsel_sb[c][:], headsel[c * 128:(c + 1) * 128, :], f"hsel{c}")
    p.v(lambda e: e.memset(spacer[:], 0.0), writes=("spacerx",))
    p.v(lambda e: e.memset(ones512[:], 1.0), writes=("ones512",))
    p.v(lambda e: e.memset(zrow[:], 0.0), writes=("zrow",))
    for c in range(NC):
        ld(x_st[c][:], xqT[c * 128:(c + 1) * 128, :], f"x{c}")
    p.barrier()
    p.v(lambda e: e.memset(zpad, 0.0), writes=("zpad",))
    zap0 = bass.AP(xscr.tensor, 0, [[D, PADMAX], [1, D]])
    zap1 = bass.AP(xscr.tensor, (T + PADMAX) * D, [[D, PADMAX], [1, D]])
    p.dma("sync", lambda e: e.dma_start(out=zap0, in_=zpad), reads=("zpad",),
          writes=("xscr_pad",))
    p.dma("sync", lambda e: e.dma_start(out=zap1, in_=zpad), reads=("zpad",),
          writes=("xscr_pad",))
    slackap = bass.AP(wscr.tensor, T * WCOLS, [[512, 4], [1, 512]])
    p.dma("sync", lambda e: e.dma_start(out=slackap, in_=zpad[0:4, 0:512]),
          reads=("zpad",), writes=("wslack",))

    def emit_adaln(crow, wmod):
        ap = bass.AP(crow.tensor, 0, [[1, 128], [128, 8]])
        p.dma("sync", lambda e, a=ap: e.dma_start(out=ctmp[:], in_=a, allow_slow_non_contiguous=True),
              writes=("ctmp",))
        p.s(lambda e: e.activation(silu_f[:], ctmp[:], AF.Exp, scale=-1.0),
            reads=("ctmp",), writes=("silu",))
        p.v(lambda e: e.tensor_scalar_add(silu2[:], silu_f[:], 1.0),
            reads=("silu",), writes=("silu2",))
        p.vs(lambda e: e.reciprocal(silu_f[:], silu2[:]),
            reads=("silu2",), writes=("silu",))
        if taps:
            p.v(lambda e: e.tensor_copy(dbgA[:], silu_f[:]),
                reads=("silu",), writes=("dbgA",))
        p.v(lambda e: e.tensor_mul(silu2[:], silu_f[:], ctmp[:]),
            reads=("silu", "ctmp"), writes=("silu2",))
        p.v(lambda e: e.tensor_copy(silu_b[:], silu2[:]),
            reads=("silu2",), writes=("silu_b",))
        p.t(lambda e: e.matmul(pb[0][:, 0:64], lhsT=onesr_sb[:],
                               rhs=zrow[:, 0:64], start=True, stop=False),
            reads=("onesr", "zrow"), writes=(PB[0],))
        for kc in range(NC):
            par = kc % 2
            stk = (f"wbuf{3 * par}", f"wbuf{3 * par + 1}", f"wbuf{3 * par + 2}")

            def wmsl(jc, par=par):
                off = jc * 128
                return wbuf[3 * par + off // 2048][:, off % 2048:off % 2048 + 128]

            src = wmod[kc * 128:(kc + 1) * 128, :]
            for si in range(3):
                ssrc = wmod[kc * 128:(kc + 1) * 128, si * 2048:(si + 1) * 2048]
                p.dma("gpsimd", lambda e, ssrc=ssrc, par=par, si=si: e.dma_start(
                        out=wbuf[3 * par + si][:], in_=ssrc),
                      writes=(stk[si],))
            for jc in range(48):
                p.t(lambda e, jc=jc, kc=kc, wmsl=wmsl: e.matmul(
                        pb[0][:, jc:jc + 1], lhsT=wmsl(jc),
                        rhs=silu_b[:, kc:kc + 1],
                        start=False, stop=(kc == NC - 1 and jc == 47)),
                    reads=(stk[jc * 128 // 2048], "silu_b"), writes=(PB[0],))
        p.v(lambda e: e.tensor_copy(mod_sb[:, 0:48], pb[0][:, 0:48]),
            reads=(PB[0],), writes=("mod", PB[0]))
        p.v(lambda e: e.tensor_scalar_add(mod_sb[:, 48:56], mod_sb[:, 8:16], 1.0),
            reads=("mod",), writes=("mod",))
        p.v(lambda e: e.tensor_scalar_add(mod_sb[:, 56:64], mod_sb[:, 32:40], 1.0),
            reads=("mod",), writes=("mod",))
        for col in (16, 40):
            sl = slice(col, col + 8)
            gA, gB, gC = gclamp[:, 0:8], gclamp[:, 8:16], gclamp[:, 16:24]
            p.v(lambda e, sl=sl, gA=gA: e.tensor_scalar(gA, mod_sb[:, sl], 0.0,
                                                        2.0, op0=ALU.is_ge,
                                                        op1=ALU.mult),
                reads=("mod",), writes=("gclamp",))
            p.v(lambda e, gA=gA, gB=gB: e.tensor_scalar_add(gB, gA, -1.0),
                reads=("gclamp",), writes=("gclamp",))
            p.v(lambda e, sl=sl, gA=gA, gB=gB: e.tensor_mul(gA, mod_sb[:, sl], gB),
                reads=("mod", "gclamp"), writes=("gclamp",))
            p.v(lambda e, gA=gA, gC=gC: e.tensor_scalar_max(gC, gA, 0.25),
                reads=("gclamp",), writes=("gclamp",))
            p.v(lambda e, sl=sl, gB=gB, gC=gC: e.tensor_mul(mod_sb[:, sl], gC, gB),
                reads=("mod", "gclamp"), writes=("mod",))

    def emit_ln_mod(hbuf, hkey, sa_col, ca_col):
        for tt in range(4):
            sl = slice(tt * 512, (tt + 1) * 512)
            for c in range(NC):
                p.v(lambda e, c=c, sl=sl: e.tensor_copy(xcast[:], x_st[c][:, sl]),
                    reads=(f"x{c}",), writes=("xcast",))
                p.v(lambda e: e.tensor_mul(xsq[:], xcast[:], xcast[:]),
                    reads=("xcast",), writes=("xsq",))
                p.t(lambda e, c=c: e.matmul(pb[1][0:1, :], lhsT=onesc_sb[:],
                                            rhs=xcast[:], start=(c == 0),
                                            stop=(c == NC - 1)),
                    reads=("xcast", "onesc"), writes=(PB[1], "xcast"))
                p.t(lambda e, c=c: e.matmul(pb[2][0:1, :], lhsT=onesc_sb[:],
                                            rhs=xsq[:], start=(c == 0),
                                            stop=(c == NC - 1)),
                    reads=("xsq", "onesc"), writes=(PB[2], "xsq"))
            p.s(lambda e: e.activation(msq512, pb[1][0:1, :], AF.Square,
                                       scale=1.0 / D),
                reads=(PB[1],), writes=("msq512",))
            p.v(lambda e: e.scalar_tensor_tensor(lnrow, pb[2][0:1, :],
                                                 1.0 / D, msq512,
                                                 op0=ALU.mult, op1=ALU.subtract),
                reads=(PB[2], "msq512"), writes=("lnrow", PB[2]))
            p.s(lambda e: e.activation(lnrow, lnrow, AF.Ln,
                                       bias=acst[0:1, 0:1]),
                reads=("lnrow", "acst"), writes=("lnrow",))
            p.s(lambda e: e.activation(lnrow, lnrow, AF.Exp, scale=-0.5),
                reads=("lnrow",), writes=("lnrow",))
            p.v(lambda e: e.tensor_copy(rs512, lnrow),
                reads=("lnrow",), writes=("rs512",))
            p.v(lambda e: e.scalar_tensor_tensor(mr512, pb[1][0:1, :], 1.0 / D,
                                                 lnrow, op0=ALU.mult,
                                                 op1=ALU.mult),
                reads=(PB[1], "lnrow"), writes=("mr512", PB[1]))
            p.t(lambda e: e.matmul(pb[3][:], lhsT=onesr_sb[:], rhs=mr512,
                                   start=True, stop=True),
                reads=("mr512", "onesr"), writes=(PB[3],))
            p.v(lambda e, sl=sl: e.tensor_copy(M_bc[:, sl], pb[3][:]),
                reads=(PB[3],), writes=("M_bc",))
            p.t(lambda e: e.matmul(pb[3][:], lhsT=onesr_sb[:], rhs=rs512,
                                   start=True, stop=True),
                reads=("rs512", "onesr"), writes=(PB[3],))
            p.v(lambda e, sl=sl: e.tensor_copy(R_bc[:, sl], pb[3][:]),
                reads=(PB[3],), writes=("R_bc",))
        for c in range(NC):
            p.v(lambda e, c=c: e.tensor_mul(hbuf[c][:], x_st[c][:], R_bc[:]),
                reads=(f"x{c}", "R_bc"), writes=(hkey(c),))
            p.v(lambda e, c=c: e.tensor_sub(hbuf[c][:], hbuf[c][:], M_bc[:]),
                reads=(hkey(c), "M_bc"), writes=(hkey(c),))
            p.v(lambda e, c=c: e.tensor_scalar(
                    hbuf[c][:], hbuf[c][:],
                    mod_sb[:, ca_col + c:ca_col + c + 1],
                    mod_sb[:, sa_col + c:sa_col + c + 1],
                    op0=ALU.mult, op1=ALU.add),
                reads=(hkey(c), "mod"), writes=(hkey(c),))

    emit_adaln(c0row, wmod0)
    if taps:
        p.dma("sync", lambda e: e.dma_start(out=taps_d["t_dbg"][:, 0:64],
                                            in_=mod_sb[:, 0:64]),
              reads=("mod",), writes=("tapdbg",))
        p.dma("sync", lambda e: e.dma_start(out=taps_d["t_dbg"][:, 64:72],
                                            in_=silu2[:]),
              reads=("silu2",), writes=("tapdbg",))
        p.dma("sync", lambda e: e.dma_start(out=taps_d["t_dbg"][:, 72:80],
                                            in_=dbgA[:]),
              reads=("dbgA",), writes=("tapdbg",))
    p.barrier()
    emit_ln_mod(bufA, lambda c: f"bufA{c}", 0, 48)

    def winT_sl(kc):
        return wbuf[kc // 2][:, (kc % 2) * D:(kc % 2 + 1) * D]

    def wattnT_sl(kc):
        return wbuf[4 + kc // 4][:, (kc % 4) * 512:(kc % 4) * 512 + WCOLS]

    def woutT_sl(kc):
        return bufB[kc // 2][:, (kc % 2) * D:(kc % 2 + 1) * D]

    for kc in range(NC):
        p.dma("gpsimd", lambda e, kc=kc: e.dma_start(
                out=winT_sl(kc), in_=WinT[kc * 128:(kc + 1) * 128, :]),
              writes=(f"wbuf{kc // 2}",))
        p.dma("gpsimd", lambda e, kc=kc: e.dma_start(
                out=wattnT_sl(kc), in_=WattnT[kc * 128:(kc + 1) * 128, :]),
              writes=(f"wbuf{4 + kc // 4}",))
        p.dma("gpsimd", lambda e, kc=kc: e.dma_start(
                out=woutT_sl(kc), in_=WoutT[kc * 128:(kc + 1) * 128, :]),
              writes=(f"bufB{kc // 2}",))

    ld(sw_sb[:], swin, "sw")
    p.vs(lambda e: e.reduce_max(swtmp[:, 0:1], sw_sb[:], axis=AX.X, negate=True),
        reads=("sw",), writes=("swt",))
    p.v(lambda e: e.tensor_scalar_add(swf2[:], sw_sb[:], swtmp[:, 0:1]),
        reads=("sw", "swt"), writes=("swf2",))
    p.s(lambda e: e.activation(swf2[:], swf2[:], AF.Exp),
        reads=("swf2",), writes=("swf2",))
    p.vs(lambda e: e.reduce_sum(swtmp[:, 0:1], swf2[:], axis=AX.X),
        reads=("swf2",), writes=("swt",))
    p.v(lambda e: e.tensor_copy(spacer[:, 0:1], spacer[:, 1:2]),
        reads=(), writes=("spacerx",))
    p.vs(lambda e: e.reciprocal(swtmp[:, 1:2], swtmp[:, 0:1]),
        reads=("swt",), writes=("swt",))
    p.v(lambda e: e.tensor_scalar(sw_sb[:], swf2[:], swtmp[:, 1:2], None,
                                  op0=ALU.mult),
        reads=("swf2", "swt"), writes=("sw",))
    p.v(lambda e: e.tensor_copy(swb[:], sw_sb[:]), reads=("sw",), writes=("swb",))
    for i in range(2):
        p.t(lambda e, i=i: e.matmul(pb[0][:, 0:1], lhsT=onesr_sb[:],
                                    rhs=swb[:, i:i + 1], start=True, stop=True),
            reads=("swb", "onesr"), writes=(PB[0],))
        p.v(lambda e, i=i: e.tensor_copy(swcol[:, i:i + 1], pb[0][:, 0:1]),
            reads=(PB[0],), writes=("swcol", PB[0]))
    for i, cw in enumerate(CTX):
        p.v(lambda e, i=i, cw=cw: e.tensor_scalar(
                masks_sc[cw][:], mask_sb[cw][:], swcol[:, i:i + 1], None,
                op0=ALU.mult),
            reads=(f"mask{cw}", "swcol"), writes=(f"mask{cw}",))

    for qt in range(NT):
        qsl = slice(qt * 128, (qt + 1) * 128)
        for oc in range(NC):
            bank, bk = pb[oc % 2], PB[oc % 2]
            for kc in range(NC):
                p.t(lambda e, oc=oc, kc=kc, qsl=qsl, bank=bank: e.matmul(
                        bank[:, 0:128],
                        lhsT=winT_sl(kc)[:, oc * 128:(oc + 1) * 128],
                        rhs=bufA[kc][:, qsl], start=(kc == 0), stop=(kc == NC - 1)),
                    reads=(f"wbuf{kc // 2}", f"bufA{kc}"), writes=(bk,))
            p.s(lambda e, oc=oc, qsl=qsl, bank=bank: e.activation(
                    bufC[oc][:, qsl], bank[:, 0:128], AF.Relu),
                reads=(bk,), writes=(f"bufC{oc}",))
            p.v(lambda e, bank=bank: e.tensor_copy(xins[:], bank[:, 0:128]),
                reads=(bk,), writes=("xins", bk))
            p.t(lambda e, oc=oc: e.transpose(pt[0][:, oc * 128:(oc + 1) * 128],
                                             xins[:], id_sb[:]),
                reads=("xins", "id_sb"), writes=(PT[0], "xins"))
        p.v(lambda e: e.tensor_copy(xtm[:], pt[0][:]),
            reads=(PT[0],), writes=("xtm", PT[0]))
        dap = bass.AP(xscr.tensor, (PADMAX + qt * 128) * D, [[D, 128], [1, D]])
        p.dma("sync", lambda e, dap=dap: e.dma_start(out=dap, in_=xtm[:]),
              reads=("xtm",), writes=(f"xscr{qt}", "xtm"))

    if taps:
        for c in range(NC):
            p.dma("sync", lambda e, c=c: e.dma_start(
                    out=taps_d["t_h"][c * 128:(c + 1) * 128, :], in_=bufA[c][:]),
                  reads=(f"bufA{c}",), writes=("taph",))
            p.dma("sync", lambda e, c=c: e.dma_start(
                    out=taps_d["t_xin"][c * 128:(c + 1) * 128, :], in_=bufC[c][:]),
                  reads=(f"bufC{c}",), writes=("tapxin",))

    def seg3(apb, cw, base):
        return apb[:, base:base + 8 * cw].rearrange("p (s c) -> p s c", c=cw)

    for qt in range(NT):
        qsl = slice(qt * 128, (qt + 1) * 128)
        for kc in range(NC):
            p.t(lambda e, kc=kc, qsl=qsl: e.matmul(
                    pb[2][:, 0:WCOLS], lhsT=bufC[kc][:, qsl], rhs=wattnT_sl(kc),
                    start=(kc == 0), stop=(kc == NC - 1)),
                reads=(f"bufC{kc}", f"wbuf{4 + kc // 4}"), writes=(PB[2],))
        p.v(lambda e: e.tensor_copy(wt_f, pb[2][:, 0:WCOLS]),
            reads=(PB[2],), writes=("wt_f", PB[2]))
        for i, cw in enumerate(CTX):
            for hh in range(8):
                b0 = SEG_BASE[i] + hh * cw
                p.vs(lambda e, b0=b0, cw=cw, col=i * 8 + hh: e.reduce_max(
                        wmax[:, col, :], wt_f[:, b0:b0 + cw], axis=AX.X,
                        negate=True),
                    reads=("wt_f",), writes=("wmax",))
            for hh in range(8):
                b0 = SEG_BASE[i] + hh * cw
                p.v(lambda e, b0=b0, cw=cw, col=i * 8 + hh: e.tensor_scalar(
                        wt_s[:, b0:b0 + cw], wt_f[:, b0:b0 + cw],
                        wmax[:, col, :], None, op0=ALU.add),
                    reads=("wt_f", "wmax"), writes=("wt_s",))
        p.s(lambda e: e.activation(wt_s[:], wt_s[:], AF.Exp),
            reads=("wt_s",), writes=("wt_s",))
        for i, cw in enumerate(CTX):
            for hh in range(8):
                b0 = SEG_BASE[i] + hh * cw
                p.vs(lambda e, b0=b0, cw=cw, col=i * 8 + hh: e.reduce_sum(
                        wsum[:, col, :], wt_s[:, b0:b0 + cw], axis=AX.X),
                    reads=("wt_s",), writes=("wsum",))
        p.v(lambda e: e.tensor_copy(spacer[:, 0:1], spacer[:, 1:2]),
            reads=(), writes=("spacerx",))
        p.vs(lambda e: e.reciprocal(wsum[:].squeeze(2), wsum[:].squeeze(2)),
            reads=("wsum",), writes=("wsum",))
        for i, cw in enumerate(CTX):
            for hh in range(8):
                b0 = SEG_BASE[i] + hh * cw
                p.v(lambda e, b0=b0, cw=cw, col=i * 8 + hh: e.tensor_scalar(
                        wt_b[:, b0:b0 + cw], wt_s[:, b0:b0 + cw],
                        wsum[:, col, :], None, op0=ALU.mult),
                    reads=("wt_s", "wsum"), writes=("wt_b",))
        dap = bass.AP(wscr.tensor, qt * 128 * WCOLS, [[WCOLS, 128], [1, WCOLS]])
        p.dma("sync", lambda e, dap=dap: e.dma_start(out=dap, in_=wt_b),
              reads=("wt_b",), writes=(f"wscr{qt}", "wt_b"))
        if taps and qt == 0:
            p.dma("sync", lambda e: e.dma_start(out=taps_d["t_w"], in_=wt_b),
                  reads=("wt_b",), writes=("tapw",))

    p.barrier()
    for qt in range(NT):
        q0 = qt * 128
        skap = bass.AP(wscr.tensor, q0 * WCOLS, [[WCOLS - 1, 128], [1, 640]])
        p.dma("sync", lambda e, skap=skap: e.dma_start(out=aall, in_=skap),
              reads=(f"wscr{qt}", "wslack"), writes=("aall",))
        rsrc = ("xscr_pad",) + tuple(f"xscr{i}"
                                     for i in range(max(0, qt - 1), min(NT, qt + 2)))
        for cw in CTX:
            pad = (cw - 1) // 2
            r0 = q0 + PADMAX - pad
            a1 = bass.AP(xscr.tensor, r0 * D, [[D, 128], [1, D]])
            a2 = bass.AP(xscr.tensor, (r0 + 128) * D, [[D, cw - 1], [1, D]])
            p.dma("sync", lambda e, a1=a1, cw=cw: e.dma_start(
                    out=xw1[cw], in_=a1), reads=rsrc, writes=(f"xw1_{cw}",))
            p.dma("sync", lambda e, a2=a2, cw=cw: e.dma_start(
                    out=xw2[cw], in_=a2), reads=rsrc, writes=(f"xw2_{cw}",))
        for h in range(8):
            for si, cw in enumerate(CTX):
                base = SEG_BASE[si] + h * cw
                wid = cw + 127
                ptb, ptk = pt[si], PT[si]
                p.v(lambda e, base=base, cw=cw, wid=wid: e.tensor_mul(
                        astage[:, 0:wid], aall[:, base:base + wid],
                        masks_sc[cw][:, 0:wid]),
                    reads=("aall", f"mask{cw}"), writes=("astage",))
                p.t(lambda e, ptb=ptb: e.transpose(ptb[:, 0:128],
                                                   astage[:, 0:128], id_sb[:]),
                    reads=("astage", "id_sb"), writes=(ptk,))
                p.t(lambda e, ptb=ptb, cw=cw, wid=wid: e.transpose(
                        ptb[0:cw - 1, 128:256], astage[:, 128:wid], id_sb[:]),
                    reads=("astage", "id_sb"), writes=(ptk, "astage"))
                p.v(lambda e, ptb=ptb: e.tensor_copy(at1, ptb[:, 0:128]),
                    reads=(ptk,), writes=("at1",))
                p.v(lambda e, ptb=ptb, cw=cw: e.tensor_copy(
                        at2[0:cw - 1, :], ptb[0:cw - 1, 128:256]),
                    reads=(ptk,), writes=("at2", ptk))
                hs = slice(h * 128, (h + 1) * 128)
                p.t(lambda e, cw=cw, hs=hs, si=si: e.matmul(
                        pb[4][:, 0:128], lhsT=xw1[cw][:, hs], rhs=at1,
                        start=(si == 0), stop=False),
                    reads=(f"xw1_{cw}", "at1"), writes=(PB[4], "at1"))
                p.t(lambda e, cw=cw, hs=hs, si=si: e.matmul(
                        pb[4][:, 0:128], lhsT=xw2[cw][:, hs],
                        rhs=at2[0:cw - 1, :], start=False, stop=(si == 1)),
                    reads=(f"xw2_{cw}", "at2"), writes=(PB[4], "at2"))
            p.v(lambda e, h=h, q0=q0: e.tensor_copy(
                    bufC[h][:, q0:q0 + 128], pb[4][:, 0:128]),
                reads=(PB[4],), writes=(f"bufC{h}", PB[4]))

    for oc in range(NC):
        for tc in range(4):
            sl = slice(tc * 512, (tc + 1) * 512)
            bank, bk = pb[(oc * 4 + tc) % 2], PB[(oc * 4 + tc) % 2]
            for kc in range(NC):
                p.t(lambda e, oc=oc, kc=kc, sl=sl, bank=bank: e.matmul(
                        bank[:], lhsT=woutT_sl(kc)[:, oc * 128:(oc + 1) * 128],
                        rhs=bufC[kc][:, sl], start=(kc == 0), stop=(kc == NC - 1)),
                    reads=(f"bufB{kc // 2}", f"bufC{kc}"), writes=(bk,))
            p.v(lambda e, oc=oc, sl=sl, bank=bank: e.scalar_tensor_tensor(
                    x_st[oc][:, sl], bank[:], mod_sb[:, 16 + oc:17 + oc],
                    x_st[oc][:, sl], op0=ALU.mult, op1=ALU.add),
                reads=(bk, "mod", f"x{oc}"), writes=(f"x{oc}", bk))

    if taps:
        for c in range(NC):
            p.dma("sync", lambda e, c=c: e.dma_start(
                    out=taps_d["t_x0"][c * 128:(c + 1) * 128, :], in_=x_st[c][:]),
                  reads=(f"x{c}",), writes=("tapx0",))
    if debug_stop == "B":
        finish(p, es, x_st, pb, PB, idf_sb, otile, yout, taps)
        return nc, es

    p.barrier()
    emit_ln_mod(bufA, lambda c: f"bufA{c}", 24, 56)

    for kc in range(NC):
        p.dma("gpsimd", lambda e, kc=kc: e.dma_start(
                out=gwT_sb[:, kc * 8:(kc + 1) * 8],
                in_=gwT[kc * 128:(kc + 1) * 128, :]),
              writes=("gwT",))
    p.dma("gpsimd", lambda e: e.dma_start(out=gb_bc[:], in_=gbrow),
          writes=("gb_bc",))
    for tt in range(NT):
        qsl = slice(tt * 128, (tt + 1) * 128)
        for kc in range(NC):
            p.t(lambda e, kc=kc, qsl=qsl: e.matmul(
                    pb[2][:, 0:8], lhsT=bufA[kc][:, qsl],
                    rhs=gwT_sb[:, kc * 8:(kc + 1) * 8],
                    start=(kc == 0), stop=False),
                reads=(f"bufA{kc}", "gwT"), writes=(PB[2],))
        p.t(lambda e: e.matmul(pb[2][:, 0:8], lhsT=onesr_sb[:], rhs=gb_bc[:],
                               start=False, stop=True),
            reads=("onesr", "gb_bc"), writes=(PB[2],))
        p.v(lambda e: e.tensor_copy(gtile[:], pb[2][:, 0:8]),
            reads=(PB[2],), writes=("gtile", PB[2]))
        p.vs(lambda e: e.reduce_max(gmax[:], gtile[:], axis=AX.X, negate=True),
            reads=("gtile",), writes=("gmax",))
        p.v(lambda e: e.tensor_scalar_add(gsel[:], gtile[:], gmax[:]),
            reads=("gtile", "gmax"), writes=("gsel",))
        p.s(lambda e: e.activation(gsel[:], gsel[:], AF.Exp),
            reads=("gsel",), writes=("gsel",))
        p.vs(lambda e: e.reduce_sum(gvs[:], gsel[:], axis=AX.X),
            reads=("gsel",), writes=("gvs",))
        p.v(lambda e: e.tensor_copy(spacer[:, 0:1], spacer[:, 1:2]),
            reads=(), writes=("spacerx",))
        p.vs(lambda e: e.reciprocal(gvs[:], gvs[:]),
            reads=("gvs",), writes=("gvs",))
        p.v(lambda e: e.tensor_scalar(gtile[:], gsel[:], gvs[:], None,
                                      op0=ALU.mult),
            reads=("gsel", "gvs"), writes=("gtile",))
        p.v(lambda e: e.memset(grank[:], 0.0), writes=("grank",))
        for ee in range(8):
            p.v(lambda e, ee=ee: e.scalar_tensor_tensor(
                    grank[:], gtile[:], gtile[:, ee:ee + 1], grank[:],
                    op0=ALU.is_lt, op1=ALU.add),
                reads=("gtile", "grank"), writes=("grank",))
        p.v(lambda e: e.tensor_scalar(gsel[:], grank[:], 2.0, None,
                                      op0=ALU.is_lt),
            reads=("grank",), writes=("gsel",))
        p.v(lambda e: e.tensor_mul(gclamp[:, 0:8], gsel[:], gtile[:]),
            reads=("gsel", "gtile"), writes=("gclamp",))
        p.vs(lambda e: e.reduce_sum(gvs[:], gclamp[:, 0:8], axis=AX.X),
            reads=("gclamp",), writes=("gvs",))
        p.v(lambda e, tt=tt: e.tensor_scalar(
                Gall[:, tt * 8:(tt + 1) * 8], gsel[:], gvs[:], None,
                op0=ALU.mult),
            reads=("gsel", "gvs"), writes=("Gall",))
        p.vs(lambda e, tt=tt: e.reduce_sum(gvs[:], Gall[:, tt * 8:(tt + 1) * 8],
                                           axis=AX.X),
            reads=("Gall",), writes=("gvs",))
        p.t(lambda e: e.transpose(pb[3][0:1, 0:128], gvs[:], idf_sb[:]),
            reads=("gvs", "idf"), writes=(PB[3],))
        p.v(lambda e, tt=tt: e.tensor_copy(gsr_bf[:, tt * 128:(tt + 1) * 128],
                                           pb[3][0:1, 0:128]),
            reads=(PB[3],), writes=("gsr", PB[3]))
        if taps and tt == 0:
            p.dma("sync", lambda e: e.dma_start(out=taps_d["t_g"], in_=Gall[:]),
                  reads=("Gall",), writes=("tapg",))

    for kc in range(NC):
        p.dma("gpsimd", lambda e, kc=kc: e.dma_start(
                out=lAa[:, kc * 96:(kc + 1) * 96],
                in_=lAspA[kc * 128:(kc + 1) * 128, :]), writes=("lAa",))
        p.dma("gpsimd", lambda e, kc=kc: e.dma_start(
                out=lAb[:, kc * 32:(kc + 1) * 32],
                in_=lAspB[kc * 128:(kc + 1) * 128, :]), writes=("lAb",))
    for tc in range(4):
        sl = slice(tc * 512, (tc + 1) * 512)
        for kc in range(NC):
            p.t(lambda e, kc=kc, sl=sl: e.matmul(
                    pb[2][0:96, :], lhsT=lAa[:, kc * 96:(kc + 1) * 96],
                    rhs=bufA[kc][:, sl], start=(kc == 0), stop=(kc == NC - 1)),
                reads=("lAa", f"bufA{kc}"), writes=(PB[2],))
        p.v(lambda e, sl=sl: e.tensor_copy(sA[:, sl], pb[2][0:96, :]),
            reads=(PB[2],), writes=("sA", PB[2]))
        for kc in range(NC):
            p.t(lambda e, kc=kc, sl=sl: e.matmul(
                    pb[2][0:32, :], lhsT=lAb[:, kc * 32:(kc + 1) * 32],
                    rhs=bufA[kc][:, sl], start=(kc == 0), stop=(kc == NC - 1)),
                reads=("lAb", f"bufA{kc}"), writes=(PB[2],))
        p.v(lambda e, sl=sl: e.tensor_copy(sB[:, sl], pb[2][0:32, :]),
            reads=(PB[2],), writes=("sB", PB[2]))

    boap = bass.AP(bo_in.tensor, 0, [[1, 128], [128, 8]])
    p.dma("sync", lambda e: e.dma_start(out=bo_fm[:], in_=boap, allow_slow_non_contiguous=True), writes=("bo_fm",))
    p.v(lambda e: e.tensor_mul(gbbo[:], bo_fm[:], mod_sb[:, 40:48]),
        reads=("bo_fm", "mod"), writes=("gbbo",))
    p.t(lambda e: e.transpose(pb[3][0:8, 0:128], gbbo[:], idf_sb[:]),
        reads=("gbbo", "idf"), writes=(PB[3],))
    p.v(lambda e: e.tensor_copy(gbboT[:], pb[3][0:8, 0:128]),
        reads=(PB[3],), writes=("gbboT", PB[3]))
    p.dma("gpsimd", lambda e: e.dma_start(out=gbbo_fl, in_=gbboT[:]),
          reads=("gbboT",), writes=("gbbo_fl",))

    def wiTq_sl(kc):
        return wbuf[kc // 2][:, (kc % 2) * 1024:(kc % 2 + 1) * 1024]

    def woTq_sl(j):
        return bufB[j // 2][:, (j % 2) * 1024:(j % 2 + 1) * 1024]

    btkeys = {"eA": "bteA", "oA": "btoA", "eB": "bteB", "oB": "btoB"}
    for Q in range(4):
        p.barrier()
        fq = slice(Q * 1024, (Q + 1) * 1024)
        for kc in range(NC):
            p.dma("gpsimd", lambda e, kc=kc, fq=fq: e.dma_start(
                    out=wiTq_sl(kc), in_=wiT[kc * 128:(kc + 1) * 128, fq]),
                  writes=(f"wbuf{kc // 2}",))
        for j in range(8):
            p.dma("gpsimd", lambda e, j=j, Q=Q: e.dma_start(
                    out=woTq_sl(j),
                    in_=woT[Q * 1024 + j * 128:Q * 1024 + (j + 1) * 128, :]),
                  writes=(f"bufB{j // 2}",))
        for k in btq:
            p.dma("gpsimd", lambda e, k=k, fq=fq: e.dma_start(
                    out=btq[k], in_=btsrc_in[k][:, fq]), writes=(btkeys[k],))
        p.dma("gpsimd", lambda e, fq=fq: e.dma_start(out=biq, in_=bi_in[:, fq]),
              writes=("biq",))
        for tt in range(NT):
            qsl = slice(tt * 128, (tt + 1) * 128)
            for yb in (3, 4):
                p.t(lambda e, yb=yb: e.matmul(pb[yb][:], lhsT=onesr_sb[:],
                                              rhs=zrow[:], start=True, stop=False),
                    reads=("onesr", "zrow"), writes=(PB[yb],))
            if Q == 0:
                for oc in range(NC):
                    yb = 3 + oc // 4
                    p.t(lambda e, oc=oc, tt=tt, yb=yb: e.matmul(
                            pb[yb][:, (oc % 4) * 128:(oc % 4 + 1) * 128],
                            lhsT=gbbo_fl[:, oc * 128:(oc + 1) * 128],
                            rhs=gsr_bf[:, tt * 128:(tt + 1) * 128],
                            start=False, stop=False),
                        reads=("gbbo_fl", "gsr"), writes=(PB[yb],))
            for f2 in range(2):
                fsl = slice(f2 * 512, (f2 + 1) * 512)
                for kc in range(NC):
                    p.t(lambda e, kc=kc, qsl=qsl, fsl=fsl: e.matmul(
                            pb[0][:], lhsT=bufA[kc][:, qsl],
                            rhs=wiTq_sl(kc)[:, fsl],
                            start=(kc == 0), stop=False),
                        reads=(f"bufA{kc}", f"wbuf{kc // 2}"), writes=(PB[0],))
                p.t(lambda e, fsl=fsl: e.matmul(
                        pb[0][:], lhsT=onesr_sb[:], rhs=biq[:, fsl],
                        start=False, stop=True),
                    reads=("onesr", "biq"), writes=(PB[0],))
                p.v(lambda e: e.tensor_copy(basebf, pb[0][:]),
                    reads=(PB[0],), writes=("basebf", PB[0]))
                for ee in range(8):
                    pp = ee // 2
                    if pp < 3:
                        ssrc, skey, po = sA, "sA", pp * 32
                        bt = btq["eA" if ee % 2 == 0 else "oA"]
                        btk = btkeys["eA" if ee % 2 == 0 else "oA"]
                    else:
                        ssrc, skey, po = sB, "sB", 0
                        bt = btq["eB" if ee % 2 == 0 else "oB"]
                        btk = btkeys["eB" if ee % 2 == 0 else "oB"]
                    p.t(lambda e, ssrc=ssrc, po=po, qsl=qsl, bt=bt, fsl=fsl:
                        e.matmul(pb[1][:], lhsT=ssrc[po:po + 32, qsl],
                                 rhs=bt[po:po + 32, fsl], start=True, stop=False),
                        reads=(skey, btk), writes=(PB[1],))
                    p.t(lambda e: e.matmul(pb[1][:], lhsT=id_sb[:], rhs=basebf,
                                           start=False, stop=True),
                        reads=("id_sb", "basebf"), writes=(PB[1],))
                    p.s(lambda e, tt=tt, ee=ee: e.activation(
                            tmpbf, pb[1][:], AF.Relu,
                            scale=Gall[:, tt * 8 + ee:tt * 8 + ee + 1]),
                        reads=(PB[1], "Gall"), writes=("tmpbf", PB[1]))
                    p.t(lambda e, ee=ee: e.matmul(pb[2][:], lhsT=id_sb[:],
                                                  rhs=tmpbf,
                                                  start=(ee == 0), stop=(ee == 7)),
                        reads=("id_sb", "tmpbf"), writes=(PB[2],))
                p.v(lambda e: e.tensor_copy(accbf, pb[2][:]),
                    reads=(PB[2],), writes=("accbf", PB[2]))
                for j4 in range(4):
                    p.t(lambda e, j4=j4: e.transpose(
                            pt[0][:, j4 * 128:(j4 + 1) * 128],
                            accbf[:, j4 * 128:(j4 + 1) * 128], id_sb[:]),
                        reads=("accbf", "id_sb"), writes=(PT[0],))
                p.v(lambda e: e.tensor_copy(accfm, pt[0][:, 0:512]),
                    reads=(PT[0],), writes=("accfm", PT[0]))
                for oc in range(NC):
                    yb = 3 + oc // 4
                    for j4 in range(4):
                        j = f2 * 4 + j4
                        last = (f2 == 1 and j4 == 3 and oc in (3, 7))
                        p.t(lambda e, oc=oc, j=j, j4=j4, yb=yb, last=last:
                            e.matmul(pb[yb][:, (oc % 4) * 128:(oc % 4 + 1) * 128],
                                     lhsT=woTq_sl(j)[:, oc * 128:(oc + 1) * 128],
                                     rhs=accfm[:, j4 * 128:(j4 + 1) * 128],
                                     start=False, stop=last),
                            reads=(f"bufB{j // 2}", "accfm"), writes=(PB[yb],))
            for oc in range(NC):
                yb = 3 + oc // 4
                p.v(lambda e, oc=oc, qsl=qsl, yb=yb: e.scalar_tensor_tensor(
                        x_st[oc][:, qsl],
                        pb[yb][:, (oc % 4) * 128:(oc % 4 + 1) * 128],
                        mod_sb[:, 40 + oc:41 + oc], x_st[oc][:, qsl],
                        op0=ALU.mult, op1=ALU.add),
                    reads=(PB[yb], "mod", f"x{oc}"), writes=(f"x{oc}", PB[yb]))

    if taps:
        for c in range(NC):
            p.dma("sync", lambda e, c=c: e.dma_start(
                    out=taps_d["t_x1"][c * 128:(c + 1) * 128, :], in_=x_st[c][:]),
                  reads=(f"x{c}",), writes=("tapx1",))
    if debug_stop == "C":
        finish(p, es, x_st, pb, PB, idf_sb, otile, yout, taps)
        return nc, es

    p.barrier()
    emit_adaln(c1row, wmod1)
    p.barrier()
    emit_ln_mod(bufA, lambda c: f"bufA{c}", 0, 48)

    for k in lqk:
        ld(lqk_sb[k][:], lqk[k], k)
    for i, (a, b_) in enumerate((("lq1", "lk1"), ("lq2", "lk2"))):
        p.v(lambda e, a=a, b_=b_: e.tensor_mul(ltmp[:], lqk_sb[a][:],
                                               lqk_sb[b_][:]),
            reads=(a, b_), writes=("ltmp",))
        p.vs(lambda e, i=i: e.reduce_sum(ldot[:, i:i + 1], ltmp[:], axis=AX.X),
            reads=("ltmp",), writes=("ldot",))
    p.s(lambda e: e.activation(ldot[:], ldot[:], AF.Exp),
        reads=("ldot",), writes=("ldot",))
    p.v(lambda e: e.tensor_sub(nred[:], ldot[:, 0:1], ldot[:, 1:2]),
        reads=("ldot",), writes=("nred",))
    p.v(lambda e: e.tensor_scalar_add(lam_col[:], nred[:], LAM_INIT),
        reads=("nred",), writes=("lam_col",))
    p.t(lambda e: e.transpose(pb[5][0:1, 0:16], lam_col[:], idf_sb[0:16, 0:16]),
        reads=("lam_col", "idf"), writes=(PB[5],))
    p.v(lambda e: e.tensor_copy(lamr[:], pb[5][0:1, 0:16]),
        reads=(PB[5],), writes=("lamr", PB[5]))


    projk = ("q1", "q2", "k1", "k2")
    WSLOT = {"q1": 0, "q2": 1, "k1": 2, "k2": 3}

    for pc in range(8):
        for name in projk:
            slot = WSLOT[name]
            for kc in range(NC):
                p.dma("gpsimd", lambda e, name=name, kc=kc, pc=pc, slot=slot:
                      e.dma_start(
                        out=wbuf[slot][:, kc * 128:(kc + 1) * 128],
                        in_=WQK[name][kc * 128:(kc + 1) * 128,
                                      pc * 128:(pc + 1) * 128]),
                      writes=(f"wbuf{slot}",))
        for kc in range(NC):
            p.dma("gpsimd", lambda e, kc=kc, pc=pc: e.dma_start(
                    out=wbuf[4][:, kc * 256:(kc + 1) * 256],
                    in_=Wv[kc * 128:(kc + 1) * 128, pc * 256:(pc + 1) * 256]),
                  writes=("wbuf4",))
        for pi, name in enumerate(projk):
            slot = WSLOT[name]
            for tc in range(4):
                sl = slice(tc * 512, (tc + 1) * 512)
                for kc in range(NC):
                    p.t(lambda e, kc=kc, sl=sl, slot=slot: e.matmul(
                            pb[0][:], lhsT=wbuf[slot][:, kc * 128:(kc + 1) * 128],
                            rhs=bufA[kc][:, sl], start=(kc == 0),
                            stop=(kc == NC - 1)),
                        reads=(f"wbuf{slot}", f"bufA{kc}"), writes=(PB[0],))
                p.v(lambda e, pi=pi, sl=sl: e.tensor_copy(bufC[pi][:, sl],
                                                          pb[0][:]),
                    reads=(PB[0],), writes=(f"bufC{pi}", PB[0]))
        for vc in range(2):
            for tc in range(4):
                sl = slice(tc * 512, (tc + 1) * 512)
                for kc in range(NC):
                    p.t(lambda e, kc=kc, sl=sl, vc=vc: e.matmul(
                            pb[0][:],
                            lhsT=wbuf[4][:, kc * 256 + vc * 128:
                                         kc * 256 + (vc + 1) * 128],
                            rhs=bufA[kc][:, sl], start=(kc == 0),
                            stop=(kc == NC - 1)),
                        reads=("wbuf4", f"bufA{kc}"), writes=(PB[0],))
                p.v(lambda e, vc=vc, sl=sl: e.tensor_copy(bufC[4 + vc][:, sl],
                                                          pb[0][:]),
                    reads=(PB[0],), writes=(f"bufC{4 + vc}", PB[0]))
        for vc in range(2):
            for slot in (2 * vc, 2 * vc + 1):
                p.v(lambda e, slot=slot: e.memset(bufB[slot][:], 1.0),
                    writes=(f"bufB{slot}",))
            for kt in range(NT):
                p.t(lambda e, vc=vc, kt=kt: e.transpose(
                        pt[0][:, (kt % 8) * 128:(kt % 8 + 1) * 128],
                        bufC[4 + vc][:, kt * 128:(kt + 1) * 128], id_sb[:]),
                    reads=(f"bufC{4 + vc}", "id_sb"), writes=(PT[0],))
                p.v(lambda e, vc=vc, kt=kt: e.tensor_copy(
                        vt_sl(vc, kt, 128),
                        pt[0][:, (kt % 8) * 128:(kt % 8 + 1) * 128]),
                    reads=(PT[0],), writes=(f"bufB{2 * vc + kt // 8}", PT[0]))
        for pi, name in enumerate(projk):
            for tc in range(4):
                sl = slice(tc * 512, (tc + 1) * 512)
                p.v(lambda e, pi=pi, sl=sl: e.tensor_mul(
                        sqs, bufC[pi][:, sl], bufC[pi][:, sl]),
                    reads=(f"bufC{pi}",), writes=("sqs",))
                p.t(lambda e, pc=pc: e.matmul(pb[5][0:16, :], lhsT=hsel_sb[pc][:],
                                              rhs=sqs, start=True, stop=True),
                    reads=(f"hsel{pc}", "sqs"), writes=(PB[5], "sqs"))
                p.vs(lambda e, tc=tc: e.reduce_max(nred4[:, tc:tc + 1],
                                                  pb[5][0:16, :], axis=AX.X),
                    reads=(PB[5],), writes=("nred4", PB[5]))
            p.v(lambda e: e.tensor_copy(spacer[:, 0:1], spacer[:, 1:2]),
                reads=(), writes=("spacerx",))
            p.vs(lambda e, name=name: e.reduce_max(nmax[name][:], nred4[:],
                                                  axis=AX.X),
                reads=("nred4",), writes=(f"nmax{name}",))
        for mi, (qa, ka) in enumerate((("q1", "k1"), ("q2", "k2"))):
            p.s(lambda e, qa=qa: e.activation(nred[:], nmax[qa][:], AF.Ln,
                                              bias=acst[0:16, 1:2]),
                reads=(f"nmax{qa}", "acst"), writes=("nred",))
            p.s(lambda e, ka=ka, mi=mi: e.activation(ldot[:, mi:mi + 1],
                                                     nmax[ka][:], AF.Ln,
                                                     bias=acst[0:16, 1:2]),
                reads=(f"nmax{ka}", "acst"), writes=("ldot",))
            p.v(lambda e, mi=mi: e.tensor_add(nred[:], nred[:], ldot[:, mi:mi + 1]),
                reads=("nred", "ldot"), writes=("nred",))
            p.s(lambda e: e.activation(nred[:], nred[:], AF.Exp, scale=0.5,
                                       bias=acst[0:16, 2:3]),
                reads=("nred", "acst"), writes=("nred",))
            p.t(lambda e: e.transpose(pb[5][0:1, 0:16], nred[:], idf_sb[0:16, 0:16]),
                reads=("nred", "idf"), writes=(PB[5],))
            p.v(lambda e: e.tensor_copy(mrow_[:], pb[5][0:1, 0:16]),
                reads=(PB[5],), writes=("mrow_", PB[5]))
            for hl in range(2):
                hh = 2 * pc + hl
                p.t(lambda e, hh=hh: e.matmul(pb[5][:, 0:1], lhsT=onesrf_sb[:],
                                              rhs=mrow_[:, hh:hh + 1],
                                              start=True, stop=True),
                    reads=("mrow_", "onesrf"), writes=(PB[5],))
                p.v(lambda e, mi=mi, hl=hl: e.tensor_scalar(
                        biascol[:, mi * 2 + hl:mi * 2 + hl + 1], pb[5][:, 0:1],
                        -1.0, None, op0=ALU.mult),
                    reads=(PB[5],), writes=("biascol", PB[5]))
        for hl in range(2):
            hh = 2 * pc + hl
            p.t(lambda e, hh=hh: e.matmul(pb[5][:, 0:1], lhsT=onesrf_sb[:],
                                          rhs=lamr[:, hh:hh + 1],
                                          start=True, stop=True),
                reads=("lamr", "onesrf"), writes=(PB[5],))
            p.v(lambda e, hl=hl: e.tensor_copy(lamb[:, hl:hl + 1], pb[5][:, 0:1]),
                reads=(PB[5],), writes=("lamb", PB[5]))

        for hl in range(2):
            hh = 2 * pc + hl
            hp = slice(hl * 64, (hl + 1) * 64)
            for qb in range(4):
                qlo = qb * 512
                nkt = qb * 4 + 4
                for ybk in (2, 3, 4):
                    p.t(lambda e, ybk=ybk: e.matmul(
                            pb[ybk][:, 0:512], lhsT=onesr_sb[:], rhs=zrow[:],
                            start=True, stop=False),
                        reads=("onesr", "zrow"), writes=(PB[ybk],))

                def pvsl(mi, s):
                    idx = mi * 4 + s
                    bk_ = 2 + idx // 3
                    off = (idx % 3) * 160
                    return pb[bk_], PB[bk_], off

                lastmm = {}
                for kt in range(nkt):
                    qv0 = max(0, kt * 128 - qlo)
                    for mi in range(2):
                        for s in range(qv0 // 128, 4):
                            lastmm[pvsl(mi, s)[1]] = (kt, mi, s)
                for kt in range(nkt):
                    qv0 = max(0, kt * 128 - qlo)
                    kvsl = slice(kt * 128, (kt + 1) * 128)
                    for mi in range(2):
                        qpi, kpi = (0, 2) if mi == 0 else (1, 3)
                        sbank, sbk = pb[mi], PB[mi]
                        p.t(lambda e, kpi=kpi, qpi=qpi, hp=hp, kvsl=kvsl,
                            qv0=qv0, qlo=qlo, sbank=sbank: e.matmul(
                                sbank[:, qv0:512],
                                lhsT=bufC[kpi][hp, kvsl],
                                rhs=bufC[qpi][hp, qlo + qv0:qlo + 512],
                                start=True, stop=True),
                            reads=(f"bufC{kpi}", f"bufC{qpi}"), writes=(sbk,))
                        p.s(lambda e, mi=mi, hl=hl, qv0=qv0, sbank=sbank:
                            e.activation(
                                esb[mi][:, qv0:512], sbank[:, qv0:512], AF.Exp,
                                bias=biascol[:, mi * 2 + hl:mi * 2 + hl + 1],
                                scale=0.125),
                            reads=(sbk, "biascol"), writes=(f"esb{mi}", sbk))
                        diag_s = -1
                        if kt * 128 >= qlo:
                            c0 = kt * 128 - qlo
                            diag_s = c0 // 128
                            p.v(lambda e, mi=mi, c0=c0: e.tensor_mul(
                                    estg[mi], esb[mi][:, c0:c0 + 128], tri_sb[:]),
                                reads=(f"esb{mi}", "tri"), writes=(f"estg{mi}",))
                        for s in range(qv0 // 128, 4):
                            bank, bk_, off = pvsl(mi, s)
                            is_last = lastmm[bk_] == (kt, mi, s)
                            lhs = estg[mi] if s == diag_s else \
                                esb[mi][:, s * 128:(s + 1) * 128]
                            lkey = f"estg{mi}" if s == diag_s else f"esb{mi}"
                            p.t(lambda e, lhs=lhs, hl=hl, kt=kt, bank=bank,
                                off=off, is_last=is_last: e.matmul(
                                    bank[:, off:off + 129], lhsT=lhs,
                                    rhs=vt_sl(hl, kt, 129),
                                    start=False, stop=is_last),
                                reads=(lkey, f"bufB{2 * hl + kt // 8}"),
                                writes=(bk_,))
                for s in range(4):
                    b1_, bk1, off1 = pvsl(0, s)
                    b2_, bk2, off2 = pvsl(1, s)
                    p.vs(lambda e, b1_=b1_, off1=off1: e.reciprocal(
                            inv1[:], b1_[:, off1 + 128:off1 + 129]),
                        reads=(bk1,), writes=("inv1",))
                    p.vs(lambda e, b2_=b2_, off2=off2: e.reciprocal(
                            inv2[:], b2_[:, off2 + 128:off2 + 129]),
                        reads=(bk2,), writes=("inv2",))
                    p.v(lambda e, hl=hl: e.tensor_scalar(
                            negc2[:], inv2[:], lamb[:, hl:hl + 1], -1.0,
                            op0=ALU.mult, op1=ALU.mult),
                        reads=("inv2", "lamb"), writes=("negc2",))
                    p.v(lambda e, b1_=b1_, off1=off1: e.tensor_scalar(
                            gnin, b1_[:, off1:off1 + 128], inv1[:], None,
                            op0=ALU.mult),
                        reads=(bk1, "inv1"), writes=("gnin", bk1))
                    p.v(lambda e, b2_=b2_, off2=off2: e.scalar_tensor_tensor(
                            gnin, b2_[:, off2:off2 + 128], negc2[:], gnin,
                            op0=ALU.mult, op1=ALU.add),
                        reads=(bk2, "negc2", "gnin"), writes=("gnin", bk2))
                    if taps and pc == 0 and hl == 0 and qb == 0 and s == 0:
                        p.dma("sync", lambda e: e.dma_start(out=taps_d["t_pv"],
                                                            in_=gnin),
                              reads=("gnin",), writes=("tappv", "gnin"))
                    p.vs(lambda e: e.reduce_sum(gnm[:], gnin, axis=AX.X),
                        reads=("gnin",), writes=("gnm",))
                    p.v(lambda e: e.tensor_scalar(inv1[:], gnm[:], 1.0 / 128,
                                                  None, op0=ALU.mult),
                        reads=("gnm",), writes=("inv1",))
                    p.v(lambda e: e.tensor_scalar_sub(gnd, gnin, inv1[:]),
                        reads=("gnin", "inv1"), writes=("gnd",))
                    p.v(lambda e: e.tensor_mul(gnsq, gnd, gnd),
                        reads=("gnd",), writes=("gnsq",))
                    p.vs(lambda e: e.reduce_sum(gnv[:], gnsq, axis=AX.X),
                        reads=("gnsq",), writes=("gnv",))
                    p.s(lambda e: e.activation(gnv[:], gnv[:], AF.Ln,
                                               scale=1.0 / 128,
                                               bias=acst[:, 0:1]),
                        reads=("gnv", "acst"), writes=("gnv",))
                    p.s(lambda e: e.activation(gnv[:], gnv[:], AF.Exp, scale=-0.5),
                        reads=("gnv",), writes=("gnv",))
                    p.v(lambda e: e.tensor_scalar(mgst[:], gnd, gnv[:],
                                                  1.0 - LAM_INIT, op0=ALU.mult,
                                                  op1=ALU.mult),
                        reads=("gnd", "gnv"), writes=("mgst",))
                    p.t(lambda e: e.transpose(pt[1][:, 0:128], mgst[:], id_sb[:]),
                        reads=("mgst", "id_sb"), writes=(PT[1],))
                    p.v(lambda e: e.tensor_copy(mgst[:], pt[1][:, 0:128]),
                        reads=(PT[1],), writes=("mgst", PT[1]))
                    qg = qlo + s * 128
                    p.dma("sync", lambda e, hh=hh, qg=qg: e.dma_start(
                            out=mscr[hh, :, qg:qg + 128], in_=mgst[:]),
                          reads=("mgst",), writes=(f"mscr{hh}", "mgst"))

    p.barrier()
    for hg in range(4):
        for gi in range(4):
            hh = hg * 4 + gi
            p.dma("gpsimd", lambda e, hh=hh, gi=gi: e.dma_start(
                    out=bufB[gi][:, 0:T], in_=mscr[hh]),
                  reads=(f"mscr{hh}",), writes=(f"bufB{gi}",))
            p.dma("gpsimd", lambda e, hh=hh, gi=gi: e.dma_start(
                    out=bufB[4 + gi // 2][:, (gi % 2) * D:(gi % 2 + 1) * D],
                    in_=Wo1[hh * 128:(hh + 1) * 128, :]),
                  writes=(f"bufB{4 + gi // 2}",))
        for oc in range(NC):
            for tc in range(4):
                sl = slice(tc * 512, (tc + 1) * 512)
                bank, bk = pb[(oc * 4 + tc) % 2], PB[(oc * 4 + tc) % 2]
                for gi in range(4):
                    p.t(lambda e, oc=oc, sl=sl, gi=gi, bank=bank: e.matmul(
                            bank[:],
                            lhsT=bufB[4 + gi // 2][:, (gi % 2) * D + oc * 128:
                                                   (gi % 2) * D + (oc + 1) * 128],
                            rhs=bufB[gi][:, sl], start=(gi == 0), stop=(gi == 3)),
                        reads=(f"bufB{4 + gi // 2}", f"bufB{gi}"), writes=(bk,))
                p.v(lambda e, oc=oc, sl=sl, bank=bank: e.scalar_tensor_tensor(
                        x_st[oc][:, sl], bank[:], mod_sb[:, 16 + oc:17 + oc],
                        x_st[oc][:, sl], op0=ALU.mult, op1=ALU.add),
                    reads=(bk, "mod", f"x{oc}"), writes=(f"x{oc}", bk))

    if taps:
        for c in range(NC):
            p.dma("sync", lambda e, c=c: e.dma_start(
                    out=taps_d["t_x2"][c * 128:(c + 1) * 128, :], in_=x_st[c][:]),
                  reads=(f"x{c}",), writes=("tapx2",))
    if debug_stop == "D":
        finish(p, es, x_st, pb, PB, idf_sb, otile, yout, taps)
        return nc, es

    p.barrier()
    emit_ln_mod(bufA, lambda c: f"bufA{c}", 24, 56)

    b1ap = bass.AP(b1row.tensor, 0, [[1, 128], [128, 32]])
    p.dma("sync", lambda e: e.dma_start(out=b1fm[:], in_=b1ap, allow_slow_non_contiguous=True), writes=("b1fm",))
    p.v(lambda e: e.tensor_copy(b1bf[:], b1fm[:]), reads=("b1fm",),
        writes=("b1bf",))
    b2ap = bass.AP(b2row.tensor, 0, [[1, 128], [128, 8]])
    p.dma("sync", lambda e: e.dma_start(out=b2fm[:], in_=b2ap, allow_slow_non_contiguous=True), writes=("b2fm",))
    p.t(lambda e: e.matmul(pb[4][:, 0:8], lhsT=onesr_sb[:], rhs=zrow[:, 0:8],
                           start=True, stop=False),
        reads=("onesr", "zrow"), writes=(PB[4],))
    for bh in range(2):
        for ah in range(2):
            for oc in range(32):
                i = oc % 2
                p.dma("gpsimd", lambda e, oc=oc, bh=bh, i=i: e.dma_start(
                        out=w1s[i], in_=W1g[oc * 128:(oc + 1) * 128,
                                            bh * 512:(bh + 1) * 512]),
                      writes=(f"w1s{i}",))
                p.dma("gpsimd", lambda e, oc=oc, ah=ah, i=i: e.dma_start(
                        out=w2s[i], in_=W2Tg[oc * 128:(oc + 1) * 128,
                                             ah * 512:(ah + 1) * 512]),
                      writes=(f"w2s{i}",))
                for bq in range(4):
                    p.t(lambda e, bq=bq, oc=oc, i=i: e.matmul(
                            pb[bq][:], lhsT=w1s[i][:, bq * 128:(bq + 1) * 128],
                            rhs=w2s[i], start=(oc == 0), stop=(oc == 31)),
                        reads=(f"w1s{i}", f"w2s{i}"), writes=(PB[bq],))
                if bh == 0:
                    for aq in range(4):
                        ac_ = ah * 4 + aq
                        p.t(lambda e, aq=aq, ac_=ac_, oc=oc, i=i: e.matmul(
                                pb[4][:, ac_:ac_ + 1],
                                lhsT=w2s[i][:, aq * 128:(aq + 1) * 128],
                                rhs=b1bf[:, oc:oc + 1],
                                start=False, stop=False),
                            reads=(f"w2s{i}", "b1bf"), writes=(PB[4],))
            for bq in range(4):
                bb = bh * 4 + bq
                p.v(lambda e, bb=bb, ah=ah, bq=bq: e.tensor_copy(
                        weT_sl(bb, ah * 512, (ah + 1) * 512), pb[bq][:]),
                    reads=(PB[bq],), writes=("weT", PB[bq]))
    p.t(lambda e: e.matmul(pb[4][:, 8:16], lhsT=onesr_sb[:], rhs=zrow[:, 0:8],
                           start=False, stop=True),
        reads=("onesr", "zrow"), writes=(PB[4],))
    p.v(lambda e: e.tensor_add(befc[:], pb[4][:, 0:8], b2fm[:]),
        reads=(PB[4], "b2fm"), writes=("befc", PB[4]))
    p.t(lambda e: e.transpose(pb[5][0:8, 0:128], befc[:], idf_sb[:]),
        reads=("befc", "idf"), writes=(PB[5],))
    p.v(lambda e: e.tensor_copy(befT[:], pb[5][0:8, 0:128]),
        reads=(PB[5],), writes=("befT", PB[5]))
    p.dma("gpsimd", lambda e: e.dma_start(out=bef_fl, in_=befT[:]),
          reads=("befT",), writes=("bef_fl",))

    for oc in range(NC):
        for tc in range(4):
            sl = slice(tc * 512, (tc + 1) * 512)
            bank, bk = pb[(oc * 4 + tc) % 2], PB[(oc * 4 + tc) % 2]
            for kc in range(NC):
                p.t(lambda e, oc=oc, kc=kc, sl=sl, bank=bank: e.matmul(
                        bank[:], lhsT=weT_sl(kc, oc * 128, (oc + 1) * 128),
                        rhs=bufA[kc][:, sl], start=(kc == 0), stop=False),
                    reads=("weT", f"bufA{kc}"), writes=(bk,))
            p.t(lambda e, oc=oc, bank=bank: e.matmul(
                    bank[:], lhsT=bef_fl[:, oc * 128:(oc + 1) * 128],
                    rhs=ones512[:], start=False, stop=True),
                reads=("bef_fl", "ones512"), writes=(bk,))
            p.v(lambda e, oc=oc, sl=sl, bank=bank: e.scalar_tensor_tensor(
                    x_st[oc][:, sl], bank[:], mod_sb[:, 40 + oc:41 + oc],
                    x_st[oc][:, sl], op0=ALU.mult, op1=ALU.add),
                reads=(bk, "mod", f"x{oc}"), writes=(f"x{oc}", bk))

    finish(p, es, x_st, pb, PB, idf_sb, otile, yout, taps)
    return nc, es


def finish(p, es, x_st, pb, PB, idf_sb, otile, yout, taps):
    p.barrier()
    for qt in range(NT):
        qsl = slice(qt * 128, (qt + 1) * 128)
        for c in range(NC):
            p.t(lambda e, c=c, qsl=qsl: e.transpose(pb[5][:, 0:128],
                                                    x_st[c][:, qsl], idf_sb[:]),
                reads=(f"x{c}", "idf"), writes=(PB[5],))
            p.v(lambda e, c=c: e.tensor_copy(otile[:, c * 128:(c + 1) * 128],
                                             pb[5][:, 0:128]),
                reads=(PB[5],), writes=("otile", PB[5]))
        p.dma("sync", lambda e, qt=qt: e.dma_start(
                out=yout[qt * 128:(qt + 1) * 128, :], in_=otile),
              reads=("otile",), writes=("yout", "otile"))
    tap_keys = ("tapw", "taph", "tapxin", "tapx0", "tapg", "tapx1", "tappv",
                "tapx2", "tapdbg") if taps else ()
    p.v(lambda e: e.nop(), reads=("yout",) + tap_keys, writes=())
    p.finalize(es)


def _consts():
    ident = np.eye(128, dtype=bfdt)
    identf = np.eye(128, dtype=np.float32)
    onesr = np.ones((1, 128), dtype=bfdt)
    onesrf = np.ones((1, 128), dtype=np.float32)
    onesc = np.ones((128, 1), dtype=bfdt)
    headsel = np.zeros((D, 16), dtype=bfdt)
    for h in range(16):
        headsel[h * 64:(h + 1) * 64, h] = 1
    masks = {}
    for cw in CTX:
        wid = cw + 127
        q = np.arange(128)[:, None]
        u = np.arange(wid)[None, :]
        masks[cw] = (((u - q) >= 0) & ((u - q) < cw)).astype(bfdt)
    kv = np.arange(128)[:, None]
    qq = np.arange(128)[None, :]
    tri = (qq >= kv).astype(bfdt)
    acst = np.zeros((128, 3), dtype=np.float32)
    acst[:, 0] = LN_EPS
    acst[:, 1] = 1e-30
    acst[:, 2] = math.log(0.125)
    return ident, identf, onesr, onesrf, onesc, headsel, masks, tri, acst


def make_in_maps(inputs):
    ac = np.ascontiguousarray
    f = np.asarray
    ident, identf, onesr, onesrf, onesc, headsel, masks, tri, acst = _consts()
    Wattn = f(inputs["L0_attn_W_attn"])
    wattn_used = np.concatenate([Wattn[0:168], Wattn[328:656]], axis=0)
    lA = f(inputs["L0_ffn_lora_A"]).reshape(128, D)
    lB = f(inputs["L0_ffn_lora_B"])
    bt = {}
    for grp, npairs, suf in ((0, 3, "A"), (3, 1, "B")):
        rows = npairs * 32
        ev = np.zeros((rows, 4096), dtype=np.float32)
        od = np.zeros((rows, 4096), dtype=np.float32)
        for pp in range(npairs):
            e0 = (grp + pp) * 2
            ev[pp * 32:pp * 32 + 16, :] = lB[e0].T
            od[pp * 32 + 16:pp * 32 + 32, :] = lB[e0 + 1].T
        bt["bte" + suf] = ev
        bt["bto" + suf] = od
    shared = dict(
        WinT=ac(f(inputs["L0_attn_W_in"]).T),
        WattnT=ac(wattn_used.T),
        WoutT=ac(f(inputs["L0_attn_W_out"]).T),
        swin=f(inputs["L0_attn_scale_weights"]).reshape(1, 2),
        wiT=ac(f(inputs["L0_ffn_wi"]).T),
        bi_in=f(inputs["L0_ffn_bi"]).reshape(1, -1),
        woT=ac(f(inputs["L0_ffn_wo"]).T),
        bo_in=f(inputs["L0_ffn_bo"]).reshape(1, -1),
        lAspA=ac(lA[0:96].T), lAspB=ac(lA[96:128].T),
        bteA=bt["bteA"], btoA=bt["btoA"], bteB=bt["bteB"], btoB=bt["btoB"],
        gwT=ac(f(inputs["L0_ffn_gate_w"]).T),
        gbrow=f(inputs["L0_ffn_gate_b"]).reshape(1, -1),
        wmod0=f(inputs["L0_wmod"]),
        Wq1=f(inputs["L1_attn_Wq1"]), Wq2=f(inputs["L1_attn_Wq2"]),
        Wk1=f(inputs["L1_attn_Wk1"]), Wk2=f(inputs["L1_attn_Wk2"]),
        Wv=f(inputs["L1_attn_Wv"]), Wo1=f(inputs["L1_attn_Wo"]),
        lq1=f(inputs["L1_attn_lq1"]), lk1=f(inputs["L1_attn_lk1"]),
        lq2=f(inputs["L1_attn_lq2"]), lk2=f(inputs["L1_attn_lk2"]),
        wmod1=f(inputs["L1_wmod"]),
        ident=ident, idf=identf, onesr=onesr, onesrf=onesrf, onesc=onesc, headsel=headsel,
        mask21=masks[21], mask41=masks[41], tri=tri, acst_in=acst,
    )
    shared = {k: np.ascontiguousarray(v) for k, v in shared.items()}
    xq = f(inputs["L0_attn_query"])
    L0c, L1c = f(inputs["L0_c"]), f(inputs["L1_c"])
    W1, W2 = f(inputs["L1_ffn_W1"]), f(inputs["L1_ffn_W2"])
    b1, b2 = f(inputs["L1_ffn_b1"]), f(inputs["L1_ffn_b2"])
    in_maps = []
    for b in range(8):
        m = dict(shared)
        m["xqT"] = ac(xq[b].T)
        m["c0row"] = ac(L0c[b:b + 1])
        m["c1row"] = ac(L1c[b:b + 1])
        m["W1g"] = ac(W1[b])
        m["W2Tg"] = ac(W2[b].T)
        m["b1row"] = ac(b1[b:b + 1])
        m["b2row"] = ac(b2[b:b + 1])
        in_maps.append(m)
    return in_maps


def kernel(**inputs):
    nc, es = build_program(debug_stop=DEBUG_STOP, taps=DEBUG_TAPS)
    with es:
        in_maps = make_in_maps(inputs)
        res = run_bass_kernel_spmd(nc, in_maps, core_ids=list(range(8)))
    if DEBUG_TAPS:
        kernel._taps = res.results
    return np.stack([r["y"] for r in res.results], axis=0)
